# version 04 (final)
# baseline (speedup 1.0000x reference)
import numpy as np
import ml_dtypes
from contextlib import ExitStack

import concourse.bass as bass
import concourse.mybir as mybir
from concourse.bass_utils import run_bass_kernel_spmd

F32 = mybir.dt.float32
BF16 = mybir.dt.bfloat16
U8 = mybir.dt.uint8
AF = mybir.ActivationFunctionType
ALU = mybir.AluOpType
BF = ml_dtypes.bfloat16

NCORES = 8
T = 1024
TT = 9
XW = TT * 128
KT = 16
DM = 2048
DI = 4096
NH = 64
HP = 64
NST = 128
CH = 128
NCH = 8
HID = 5632
NHT = 44
EPS = 1e-5

NC_REAL = [(128, 640), (640, 1152)]
NC_HALO = [(125, 637), (637, 1149), (1149, 1152)]

PHASES = ["p1", "p3a", "p2", "p3b", "p4", "p5b", "p5c", "p6a", "p6b", "p6c", "p6"]


class Sem:
    def __init__(self, nc, stack, name):
        self.h = stack.enter_context(nc.semaphore(name))
        self.n = 0

    def d(self, instr):
        self.n += 16
        instr.then_inc(self.h, 16)
        return (self, self.n)

    def e(self, instr):
        self.n += 1
        instr.then_inc(self.h, 1)
        return (self, self.n)

    def tok(self):
        return (self, self.n)


def wt(eng, tok):
    if tok is not None:
        eng.wait_ge(tok[0].h, tok[1])


class _EndP6(Exception):
    pass


def build(debug=False, upto="p6"):
    upto_i = PHASES.index(upto)

    def on(p):
        return PHASES.index(p) <= upto_i

    nc = bass.Bass("TRN2", target_bir_lowering=False)

    xw_d = nc.dram_tensor("xw", [XW, DM], F32, kind="ExternalInput").ap()
    winz_d = nc.dram_tensor("winz", [128, 32, KT, 128], F32, kind="ExternalInput").ap()
    winxbc_d = nc.dram_tensor("winxbc", [128, 34, KT, 128], F32, kind="ExternalInput").ap()
    windt_d = nc.dram_tensor("windt", [128, 1, KT, 64], F32, kind="ExternalInput").ap()
    wg_d = nc.dram_tensor("wg4", [128, NHT, KT, 128], U8, kind="ExternalInput").ap()
    wu_d = nc.dram_tensor("wu4", [128, NHT, KT, 128], U8, kind="ExternalInput").ap()
    wd_d = nc.dram_tensor("wd4", [128, KT, NHT, 128], U8, kind="ExternalInput").ap()
    wo_d = nc.dram_tensor("wo4", [128, KT, 32, 128], F32, kind="ExternalInput").ap()
    cwt_d = nc.dram_tensor("conv_wt", [4, 4352], F32, kind="ExternalInput").ap()
    cb_d = nc.dram_tensor("conv_b", [1, 4352], F32, kind="ExternalInput").ap()
    a_d = nc.dram_tensor("a_vec", [1, NH], F32, kind="ExternalInput").ap()
    dvec_d = nc.dram_tensor("d_vec", [1, NH], F32, kind="ExternalInput").ap()
    dtb_d = nc.dram_tensor("dt_bias", [1, NH], F32, kind="ExternalInput").ap()
    nw_d = nc.dram_tensor("norm_w", [32, 128], F32, kind="ExternalInput").ap()
    sg_d = nc.dram_tensor("s_gate", [NHT, 128], F32, kind="ExternalInput").ap()
    su_d = nc.dram_tensor("s_up", [NHT, 128], F32, kind="ExternalInput").ap()
    sd_d = nc.dram_tensor("s_down", [16, 128], F32, kind="ExternalInput").ap()
    m_d = nc.dram_tensor("mask", [1, NCORES], F32, kind="ExternalInput").ap()
    out_d = nc.dram_tensor("out", [T, DM], F32, kind="ExternalOutput").ap()

    szsp = nc.dram_tensor("szsp", [32, 128, T], BF16,
                          kind="ExternalOutput" if debug else "Internal")
    gsp = nc.dram_tensor("gsp", [DM, T], BF16,
                         kind="ExternalOutput" if debug else "Internal")
    dbg = {}

    def dbgo(name, shape, dtype=F32):
        dbg[name] = nc.dram_tensor(name, list(shape), dtype, kind="ExternalOutput").ap()
        return dbg[name]

    identb_d = nc.inline_tensor(np.eye(128, dtype=BF), "c_identb")
    identf_d = nc.inline_tensor(np.eye(128, dtype=np.float32), "c_identf")
    onesb_d = nc.inline_tensor(np.ones((1, 128), dtype=BF), "c_onesb")
    onesf_d = nc.inline_tensor(np.ones((1, 128), dtype=np.float32), "c_onesf")
    onescol_d = nc.inline_tensor(np.ones((128, 1), dtype=BF), "c_onescol")
    tri_d = nc.inline_tensor(np.triu(np.ones((128, 128), np.float32)).astype(BF), "c_tri")
    n128_d = nc.inline_tensor(np.full((1, 128), -128.0, dtype=BF), "c_n128")

    st0 = ExitStack()
    sConst = Sem(nc, st0, "sConst")
    sCst = Sem(nc, st0, "sCst")
    sX = [Sem(nc, st0, f"sX{i}") for i in range(2)]
    sIO = Sem(nc, st0, "sIO")
    sFL = [Sem(nc, st0, f"sFL{i}") for i in range(2)]
    sG = [Sem(nc, st0, f"sG{i}") for i in range(2)]
    sDB = Sem(nc, st0, "sDB")
    sWA = [Sem(nc, st0, f"sWA{i}") for i in range(2)]
    sWB = [Sem(nc, st0, f"sWB{i}") for i in range(2)]
    sZ = [Sem(nc, st0, f"sZ{i}") for i in range(2)]
    sSP = Sem(nc, st0, "sSP")
    sPE = Sem(nc, st0, "sPE")
    sV = Sem(nc, st0, "sV")
    sA = Sem(nc, st0, "sA")
    sCC = Sem(nc, st0, "sCC")
    sST = Sem(nc, st0, "sST")

    identb = st0.enter_context(nc.sbuf_tensor("identb", [128, 128], BF16))
    identf = st0.enter_context(nc.sbuf_tensor("identf", [128, 128], F32))
    onesb = st0.enter_context(nc.sbuf_tensor("onesb", [1, 128], BF16))
    onesf = st0.enter_context(nc.sbuf_tensor("onesf", [1, 128], F32))
    onescol = st0.enter_context(nc.sbuf_tensor("onescol", [128, 1], BF16))
    tri = st0.enter_context(nc.sbuf_tensor("tri", [128, 128], BF16))
    n128 = st0.enter_context(nc.sbuf_tensor("n128", [1, 128], BF16))
    cwcol = st0.enter_context(nc.sbuf_tensor("cwcol", [128, 4 * 34], F32))
    cbcol = st0.enter_context(nc.sbuf_tensor("cbcol", [128, 34], F32))
    nwcol = st0.enter_context(nc.sbuf_tensor("nwcol", [128, 32], F32))
    sgcol = st0.enter_context(nc.sbuf_tensor("sgcol", [128, NHT], F32))
    sucol = st0.enter_context(nc.sbuf_tensor("sucol", [128, NHT], F32))
    sdcol = st0.enter_context(nc.sbuf_tensor("sdcol", [128, 16], F32))
    acol = st0.enter_context(nc.sbuf_tensor("acol", [NH, 1], F32))
    dtbcol = st0.enter_context(nc.sbuf_tensor("dtbcol", [NH, 1], F32))
    dbc = st0.enter_context(nc.sbuf_tensor("dbc", [128, NH], F32))
    stXN = ExitStack()
    xnT = stXN.enter_context(nc.sbuf_tensor("xnT", [128, KT * XW], BF16, side="right"))
    csxnh = stXN.enter_context(nc.sbuf_tensor("csxnh", [1, T], BF16, side="right"))
    csxnl = stXN.enter_context(nc.sbuf_tensor("csxnl", [1, T], BF16, side="right"))

    for dst, src in ((identb, identb_d), (identf, identf_d), (onesb, onesb_d),
                     (onesf, onesf_d), (onescol, onescol_d), (tri, tri_d),
                     (n128, n128_d)):
        sConst.d(nc.sync.dma_start(dst[:, :], src.ap()))
    tok_consts = sConst.tok()


    with nc.sbuf_tensor("cstage", [NHT, 128], F32) as cstage, \
         nc.psum_tensor("pcst", [128, 512], F32) as pcst:
        wt(nc.tensor, tok_consts)

        def colify(src_ap, rows, dst, c0, ncols):
            t0 = sCst.d(nc.sync.dma_start(cstage[0:rows, :], src_ap))
            wt(nc.tensor, t0)
            t1 = sPE.e(nc.tensor.transpose(pcst[:, 0:rows], cstage[0:rows, :],
                                           identf[0:rows, 0:rows]))
            wt(nc.vector, t1)
            t2 = sV.e(nc.vector.tensor_copy(dst[:, c0:c0 + ncols], pcst[:, 0:rows]))
            wt(nc.sync, t2)
            wt(nc.tensor, t2)

        for k in range(4):
            colify(cwt_d[k:k + 1, :].rearrange("o (p c) -> p (o c)", p=34), 34,
                   cwcol, k * 34, 34)
        colify(cb_d.rearrange("o (p c) -> p (o c)", p=34), 34, cbcol, 0, 34)
        colify(nw_d, 32, nwcol, 0, 32)
        colify(sg_d, NHT, sgcol, 0, NHT)
        colify(su_d, NHT, sucol, 0, NHT)
        colify(sd_d, 16, sdcol, 0, 16)
        with nc.sbuf_tensor("r64", [1, 64], F32) as r64:
            for src, dst in ((a_d, acol), (dtb_d, dtbcol)):
                t0 = sCst.d(nc.sync.dma_start(r64[:, :], src))
                wt(nc.tensor, t0)
                t1 = sPE.e(nc.tensor.transpose(pcst[0:64, 0:1], r64[0:1, :],
                                               identf[0:1, 0:1]))
                wt(nc.vector, t1)
                t2 = sV.e(nc.vector.tensor_copy(dst[:, :], pcst[0:64, 0:1]))
                wt(nc.sync, t2)
                wt(nc.tensor, t2)
            t0 = sCst.d(nc.sync.dma_start(r64[:, :], dvec_d))
            wt(nc.tensor, t0)
            t1 = sPE.e(nc.tensor.matmul(pcst[:, 0:64], onesf[0:1, :], r64[0:1, :],
                                        start=True, stop=True))
            wt(nc.vector, t1)
            t2 = sV.e(nc.vector.tensor_copy(dbc[:, :], pcst[:, 0:64]))
            wt(nc.sync, t2)
            wt(nc.tensor, t2)

    with nc.sbuf_tensor("xt0", [128, DM], F32) as xt0, \
         nc.sbuf_tensor("xt1", [128, DM], F32) as xt1, \
         nc.sbuf_tensor("sq", [128, DM], F32) as sq, \
         nc.sbuf_tensor("xnb", [128, DM], BF16) as xnb, \
         nc.sbuf_tensor("lstat", [128, 8], F32) as lstat, \
         nc.psum_tensor("ptrA", [128, 128], BF16) as ptrA, \
         nc.psum_tensor("ptrB", [128, 128], BF16) as ptrB, \
         nc.psum_tensor("ptrC", [128, 128], BF16) as ptrC, \
         nc.psum_tensor("ptrD", [128, 128], BF16) as ptrD, \
         nc.psum_tensor("pjA", [128, 512], F32) as pjA:
        ptrs = [ptrA, ptrB, ptrC, ptrD]
        bufs = [xt0, xt1]
        buf_free = [None, None]
        slot_free = [None, None, None, None]
        for tt in range(TT):
            buf = bufs[tt % 2]
            wt(nc.sync, buf_free[tt % 2])
            t0 = sX[tt % 2].d(nc.sync.dma_start(buf[:, :], xw_d[tt * 128:(tt + 1) * 128, :]))
            wt(nc.vector, t0)
            tr = sV.e(nc.vector.tensor_reduce(lstat[:, 0:1], buf[:, :],
                                              mybir.AxisListType.X, ALU.add))
            wt(nc.scalar, t0)
            tq = sA.e(nc.scalar.activation(sq[:, :], buf[:, :], AF.Square,
                                           accum_out=lstat[:, 1:2]))
            wt(nc.vector, tq)
            wt(nc.vector, tr)
            nc.vector.tensor_scalar(lstat[:, 3:4], lstat[:, 1:2], 1.0 / DM, EPS,
                                    ALU.mult, ALU.add)
            t2 = sV.e(nc.vector.tensor_scalar(lstat[:, 2:3], lstat[:, 0:1],
                                              1.0 / DM, None, ALU.mult))
            wt(nc.vector, t2)
            t4 = sV.e(nc.vector.tensor_scalar(lstat[:, 4:5], lstat[:, 2:3],
                                              lstat[:, 2:3], None, ALU.mult))
            wt(nc.vector, t4)
            tv = sV.e(nc.vector.tensor_sub(lstat[:, 5:6], lstat[:, 3:4], lstat[:, 4:5]))
            wt(nc.scalar, tv)
            tl6 = sA.e(nc.scalar.activation(lstat[:, 6:7], lstat[:, 5:6], AF.Ln))
            wt(nc.scalar, tl6)
            ta = sA.e(nc.scalar.activation(lstat[:, 7:8], lstat[:, 6:7], AF.Exp,
                                           scale=-0.5))
            wt(nc.vector, ta)
            tx = sV.e(nc.vector.tensor_scalar(xnb[:, :], buf[:, :], lstat[:, 2:3],
                                              lstat[:, 7:8], ALU.subtract, ALU.mult))
            buf_free[tt % 2] = tx
            wt(nc.tensor, tx)
            for kt in range(KT):
                sl = kt % 4
                wt(nc.tensor, slot_free[sl])
                tm = sPE.e(nc.tensor.transpose(ptrs[sl][:, :],
                                               xnb[:, kt * 128:(kt + 1) * 128],
                                               identb[:, :]))
                wt(nc.vector, tm)
                slot_free[sl] = sV.e(nc.vector.tensor_copy(
                    xnT[:, kt * XW + tt * 128: kt * XW + (tt + 1) * 128],
                    ptrs[sl][:, :]))
        for ci, (r0, r1) in enumerate(NC_REAL):
            w = r1 - r0
            wt(nc.tensor, sV.tok())
            for kt in range(KT - 1):
                nc.tensor.matmul(pjA[0:1, 0:w], onescol[:, 0:1],
                                 xnT[:, kt * XW + r0: kt * XW + r1],
                                 start=(kt == 0), stop=False)
            tm = sPE.e(nc.tensor.matmul(pjA[0:1, 0:w], onescol[:, 0:1],
                                 xnT[:, (KT - 1) * XW + r0: (KT - 1) * XW + r1],
                                 start=False, stop=True))
            wt(nc.vector, tm)
            nc.vector.tensor_copy(csxnh[0:1, ci * 512:ci * 512 + w], pjA[0:1, 0:w])
            tl = sV.e(nc.vector.tensor_tensor(csxnl[0:1, ci * 512:ci * 512 + w],
                                              pjA[0:1, 0:w],
                                              csxnh[0:1, ci * 512:ci * 512 + w],
                                              ALU.subtract))
            wt(nc.tensor, tl)
    nc.all_engine_barrier()

    if debug:
        d = dbgo("dbg_xnT", [128, KT * XW], BF16)
        sDB.d(nc.sync.dma_start(d, xnT[:, :]))
        d = dbgo("dbg_csxn", [2, T], BF16)
        sDB.d(nc.sync.dma_start(d[0:1, :], csxnh[:, :]))
        sDB.d(nc.sync.dma_start(d[1:2, :], csxnl[:, :]))
        wt(nc.sync, sDB.tok())
        nc.all_engine_barrier()

    stSSD = ExitStack()
    dtf = stSSD.enter_context(nc.sbuf_tensor("dtf", [NH, T], F32))
    cAll = stSSD.enter_context(nc.sbuf_tensor("cAll", [NH, T], F32))
    cAhi = stSSD.enter_context(nc.sbuf_tensor("cAhi", [NH, T], BF16))
    cAlo = stSSD.enter_context(nc.sbuf_tensor("cAlo", [NH, T], BF16))
    ncAhi = stSSD.enter_context(nc.sbuf_tensor("ncAhi", [NH, T], BF16))
    ncAlo = stSSD.enter_context(nc.sbuf_tensor("ncAlo", [NH, T], BF16))
    runsumf = stSSD.enter_context(nc.sbuf_tensor("runsumf", [NH, 1], F32))

    if on("p3a"):
        with nc.sbuf_tensor("wdt", [128, KT * 64], BF16) as wdt, \
             nc.sbuf_tensor("dtraw", [NH, T], F32) as dtraw, \
             nc.sbuf_tensor("scr64", [NH, T], F32) as scr64, \
             nc.psum_tensor("pjd", [128, 512], F32) as pjd:
            tw = sWA[0].d(nc.gpsimd.dma_start(
                wdt[:, :].rearrange("p (k c) -> p k c", k=KT), windt_d[:, 0, :, :]))
            wt(nc.tensor, tw)
            ev = [None, None]
            for ci, (r0, r1) in enumerate(NC_REAL):
                w = r1 - r0
                for kt in range(KT - 1):
                    nc.tensor.matmul(pjd[0:64, 0:w], wdt[:, kt * 64:(kt + 1) * 64],
                                     xnT[:, kt * XW + r0: kt * XW + r1],
                                     start=(kt == 0), stop=False)
                tm = sPE.e(nc.tensor.matmul(pjd[0:64, 0:w],
                                     wdt[:, (KT - 1) * 64:KT * 64],
                                     xnT[:, (KT - 1) * XW + r0: (KT - 1) * XW + r1],
                                     start=False, stop=True))
                wt(nc.vector, tm)
                ev[ci] = sV.e(nc.vector.tensor_copy(dtraw[:, ci * 512:ci * 512 + w],
                                                    pjd[0:64, 0:w]))
                if ci == 0:
                    wt(nc.tensor, ev[0])
            tv = sV.e(nc.vector.tensor_scalar(dtraw[:, :], dtraw[:, :], dtbcol[:, 0:1],
                                              None, ALU.add))
            wt(nc.scalar, tv)
            nc.scalar.activation(scr64[:, :], dtraw[:, :], AF.Abs)
            nc.scalar.activation(scr64[:, :], scr64[:, :], AF.Exp, scale=-1.0)
            ta = sA.e(nc.scalar.activation(scr64[:, :], scr64[:, :], AF.Ln, bias=1.0))
            wt(nc.vector, ta)
            nc.vector.tensor_single_scalar(dtf[:, :], dtraw[:, :], 0.0, ALU.max)
            nc.vector.tensor_add(dtf[:, :], dtf[:, :], scr64[:, :])
            nc.vector.tensor_scalar(dtraw[:, :], dtf[:, :], acol[:, 0:1], None, ALU.mult)
            tsc = None
            for k in range(NCH):
                tsc = sV.e(nc.vector.tensor_tensor_scan(cAll[:, k * CH:(k + 1) * CH],
                                             dtraw[:, k * CH:(k + 1) * CH],
                                             dtraw[:, k * CH:(k + 1) * CH],
                                             0.0, ALU.add, ALU.bypass))
            wt(nc.vector, tsc)
            nc.vector.tensor_reduce(runsumf[:, :],
                                    cAll[:, CH - 1::CH], mybir.AxisListType.X, ALU.add)
            nc.vector.tensor_copy(cAhi[:, :], cAll[:, :])
            nc.vector.tensor_copy(scr64[:, :], cAhi[:, :])
            nc.vector.tensor_sub(cAlo[:, :], cAll[:, :], scr64[:, :])
            nc.vector.tensor_scalar(ncAhi[:, :], cAhi[:, :], -1.0, None, ALU.mult)
            tv2 = sV.e(nc.vector.tensor_scalar(ncAlo[:, :], cAlo[:, :], -1.0,
                                               None, ALU.mult))
            wt(nc.scalar, tv2)
        nc.all_engine_barrier()

        if debug:
            for nm, sb, dt_ in (("dbg_dtf", dtf, F32), ("dbg_cAll", cAll, F32),
                                ("dbg_runsum", runsumf, F32)):
                d = dbgo(nm, list(sb.shape), dt_)
                sDB.d(nc.sync.dma_start(d, sb[:, :]))
            wt(nc.sync, sDB.tok())
            nc.all_engine_barrier()

    if on("p2"):
        stF = ExitStack()
        act = stF.enter_context(nc.sbuf_tensor("act", [128, NHT * T], BF16))
        csacth = stF.enter_context(nc.sbuf_tensor("csacth", [1, T], BF16))
        csactl = stF.enter_context(nc.sbuf_tensor("csactl", [1, T], BF16))
        stF2 = ExitStack()
        wgb = [stF2.enter_context(nc.sbuf_tensor(f"wgb{i}", [128, 2 * KT * 128], BF16))
               for i in range(2)]
        wub = [stF2.enter_context(nc.sbuf_tensor(f"wub{i}", [128, 2 * KT * 128], BF16))
               for i in range(2)]
        sgst = [stF2.enter_context(nc.sbuf_tensor(f"sgst{i}", [128, 512], BF16))
                for i in range(2)]
        pstF = ExitStack()
        pg = [pstF.enter_context(nc.psum_tensor(f"pg{i}", [128, 512], F32)) for i in range(2)]
        pu = [pstF.enter_context(nc.psum_tensor(f"pu{i}", [128, 512], F32)) for i in range(2)]
        pcs = pstF.enter_context(nc.psum_tensor("pcs", [1, 512], F32))

        def uload(src, j2, dstbuf, sem):
            return sem.d(nc.gpsimd.dma_start(
                dstbuf[:, :].rearrange("p (j k c) -> p j k c", j=2, k=KT),
                src[:, 2 * j2:2 * j2 + 2, :, :]))

        NJ = NHT // 2
        glds = [None, None]
        ulds = [None, None]
        gfree = [None, None]
        ufree = [None, None]
        sg_free = [None, None]
        pg_free = [None, None]
        pu_free = [None, None]
        glds[0] = uload(wg_d, 0, wgb[0], sWA[0])
        ulds[0] = uload(wu_d, 0, wub[0], sWB[0])
        for j in range(NJ):
            cb = j % 2
            if j + 1 < NJ:
                wt(nc.gpsimd, gfree[(j + 1) % 2])
                glds[(j + 1) % 2] = uload(wg_d, j + 1, wgb[(j + 1) % 2],
                                          sWA[(j + 1) % 2])
                wt(nc.gpsimd, ufree[(j + 1) % 2])
                ulds[(j + 1) % 2] = uload(wu_d, j + 1, wub[(j + 1) % 2],
                                          sWB[(j + 1) % 2])
            wt(nc.tensor, glds[cb])
            wt(nc.tensor, ulds[cb])
            for mmi in range(2):
                hti = 2 * j + mmi
                for ci, (r0, r1) in enumerate(NC_REAL):
                    sslot = (mmi * 2 + ci) % 2
                    wt(nc.tensor, pg_free[sslot])
                    for kt in range(KT):
                        nc.tensor.matmul(pg[sslot][:, :],
                                         wgb[cb][:, (mmi * KT + kt) * 128:(mmi * KT + kt + 1) * 128],
                                         xnT[:, kt * XW + r0: kt * XW + r1],
                                         start=(kt == 0), stop=False)
                    nc.tensor.matmul(pg[sslot][:, :], n128[0:1, :],
                                     csxnh[0:1, ci * 512:(ci + 1) * 512],
                                     start=False, stop=False)
                    tg = sPE.e(nc.tensor.matmul(pg[sslot][:, :], n128[0:1, :],
                                                csxnl[0:1, ci * 512:(ci + 1) * 512],
                                                start=False, stop=True))
                    wt(nc.tensor, pu_free[sslot])
                    for kt in range(KT):
                        nc.tensor.matmul(pu[sslot][:, :],
                                         wub[cb][:, (mmi * KT + kt) * 128:(mmi * KT + kt + 1) * 128],
                                         xnT[:, kt * XW + r0: kt * XW + r1],
                                         start=(kt == 0), stop=False)
                    nc.tensor.matmul(pu[sslot][:, :], n128[0:1, :],
                                     csxnh[0:1, ci * 512:(ci + 1) * 512],
                                     start=False, stop=False)
                    tu = sPE.e(nc.tensor.matmul(pu[sslot][:, :], n128[0:1, :],
                                                csxnl[0:1, ci * 512:(ci + 1) * 512],
                                                start=False, stop=True))
                    wt(nc.scalar, tg)
                    wt(nc.scalar, sg_free[sslot])
                    tsl = sA.e(nc.scalar.activation(sgst[sslot][:, :], pg[sslot][:, :],
                                                    AF.Silu, scale=sgcol[:, hti:hti + 1]))
                    pg_free[sslot] = tsl
                    wt(nc.vector, tsl)
                    wt(nc.vector, tu)
                    tact = sV.e(nc.vector.scalar_tensor_tensor(
                        act[:, hti * T + ci * 512: hti * T + (ci + 1) * 512],
                        pu[sslot][:, :], sucol[:, hti:hti + 1], sgst[sslot][:, :],
                        ALU.mult, ALU.mult))
                    pu_free[sslot] = tact
                    sg_free[sslot] = tact
            gfree[cb] = sPE.tok()
            ufree[cb] = sPE.tok()
        stF2.close()
        tok_act = sV.tok()

        wt(nc.tensor, tok_act)
        for ci in range(2):
            for kk in range(NHT - 1):
                nc.tensor.matmul(pcs[0:1, :], onescol[:, 0:1],
                                 act[:, kk * T + ci * 512: kk * T + (ci + 1) * 512],
                                 start=(kk == 0), stop=False)
            tm = sPE.e(nc.tensor.matmul(pcs[0:1, :], onescol[:, 0:1],
                                 act[:, (NHT - 1) * T + ci * 512: (NHT - 1) * T + (ci + 1) * 512],
                                 start=False, stop=True))
            wt(nc.vector, tm)
            nc.vector.tensor_copy(csacth[0:1, ci * 512:(ci + 1) * 512], pcs[0:1, :])
            tl = sV.e(nc.vector.tensor_tensor(csactl[0:1, ci * 512:(ci + 1) * 512],
                                              pcs[0:1, :],
                                              csacth[0:1, ci * 512:(ci + 1) * 512],
                                              ALU.subtract))
            wt(nc.tensor, tl)

        stF3 = ExitStack()
        wdb = [stF3.enter_context(nc.sbuf_tensor(f"wdb{i}", [128, NHT * 128], BF16))
               for i in range(2)]
        gsb = [stF3.enter_context(nc.sbuf_tensor(f"gsb{i}", [128, 512], BF16))
               for i in range(2)]
        dlds = [None, None]
        dfree = [None, None]
        gs_free = [None, None]
        pd_free = [None, None]
        wt(nc.gpsimd, sPE.tok())
        dlds[0] = sWA[0].d(nc.gpsimd.dma_start(
            wdb[0][:, :].rearrange("p (k c) -> p k c", k=NHT), wd_d[:, 0, :, :]))
        for dmt in range(KT):
            cb = dmt % 2
            if dmt + 1 < KT:
                wt(nc.gpsimd, dfree[(dmt + 1) % 2])
                dlds[(dmt + 1) % 2] = sWA[(dmt + 1) % 2].d(nc.gpsimd.dma_start(
                    wdb[(dmt + 1) % 2][:, :].rearrange("p (k c) -> p k c", k=NHT),
                    wd_d[:, dmt + 1, :, :]))
            wt(nc.tensor, dlds[cb])
            for ci in range(2):
                sslot = ci % 2
                wt(nc.tensor, pd_free[sslot])
                for kk in range(NHT):
                    nc.tensor.matmul(pg[sslot][:, :], wdb[cb][:, kk * 128:(kk + 1) * 128],
                                     act[:, kk * T + ci * 512: kk * T + (ci + 1) * 512],
                                     start=(kk == 0), stop=False)
                nc.tensor.matmul(pg[sslot][:, :], n128[0:1, :],
                                 csacth[0:1, ci * 512:(ci + 1) * 512],
                                 start=False, stop=False)
                tm = sPE.e(nc.tensor.matmul(pg[sslot][:, :], n128[0:1, :],
                                            csactl[0:1, ci * 512:(ci + 1) * 512],
                                            start=False, stop=True))
                wt(nc.scalar, tm)
                wt(nc.scalar, gs_free[sslot])
                ta = sA.e(nc.scalar.activation(gsb[sslot][:, :], pg[sslot][:, :],
                                               AF.Silu, scale=sdcol[:, dmt:dmt + 1]))
                pd_free[sslot] = ta
                wt(nc.scalar, ta)
                gs_free[sslot] = sSP.d(nc.scalar.dma_start(
                    gsp.ap()[dmt * 128:(dmt + 1) * 128, ci * 512:(ci + 1) * 512],
                    gsb[sslot][:, :]))
            dfree[cb] = sPE.tok()
        wt(nc.sync, sSP.tok())
        stF3.close()
        pstF.close()
        stF.close()
        nc.all_engine_barrier()

        if debug:
            d = dbgo("dbg_csact", [2, T], BF16)
            sDB.d(nc.sync.dma_start(d[0:1, :], csacth[:, :]))
            sDB.d(nc.sync.dma_start(d[1:2, :], csactl[:, :]))
            wt(nc.sync, sDB.tok())
            nc.all_engine_barrier()

    stXY = ExitStack()
    XY = stXY.enter_context(nc.sbuf_tensor("XY", [128, NCH * DI], BF16))
    Bk = stXY.enter_context(nc.sbuf_tensor("Bk", [128, T], BF16))
    Ck = stXY.enter_context(nc.sbuf_tensor("Ck", [128, T], BF16))
    BTk = stXY.enter_context(nc.sbuf_tensor("BTk", [128, T], BF16))

    if on("p3b"):
        stB = ExitStack()
        wb = [stB.enter_context(nc.sbuf_tensor(f"wb{i}", [128, KT * 128], BF16))
              for i in range(2)]
        xb = [stB.enter_context(nc.sbuf_tensor(f"xb{i}", [128, 3 + T], F32))
              for i in range(2)]
        cv = [stB.enter_context(nc.sbuf_tensor(f"cv{i}", [128, T], F32))
              for i in range(2)]
        cvb = [stB.enter_context(nc.sbuf_tensor(f"cvb{i}", [128, T], BF16))
               for i in range(2)]
        pstB = ExitStack()
        pj = [pstB.enter_context(nc.psum_tensor(f"pj{i}", [128, 512], F32))
              for i in range(2)]
        ptrbs = [pstB.enter_context(nc.psum_tensor(f"ptrb{i}", [128, 128], BF16))
                 for i in range(4)]

        def wload34(src, ti, dstbuf, sem):
            return sem.d(nc.gpsimd.dma_start(
                dstbuf[:, :].rearrange("p (k c) -> p k c", k=KT), src[:, ti, :, :]))

        wl = [None, None]
        wfree = [None, None]
        xb_free = [None, None]
        cv_free = [None, None]
        cvb_free = [None, None]
        pj_free = [None, None]
        slotT = [None] * 4
        wt(nc.gpsimd, sPE.tok())
        wl[0] = wload34(winxbc_d, 0, wb[0], sWA[0])
        for ti in range(34):
            cb = ti % 2
            if ti + 1 < 34:
                wt(nc.gpsimd, wfree[(ti + 1) % 2])
                wl[(ti + 1) % 2] = wload34(winxbc_d, ti + 1, wb[(ti + 1) % 2],
                                           sWA[(ti + 1) % 2])
            wt(nc.tensor, wl[cb])
            wt(nc.tensor, xb_free[cb])
            for ciH, (r0, r1) in enumerate(NC_HALO):
                w = r1 - r0
                ps = pj[ciH % 2]
                wt(nc.tensor, pj_free[ciH % 2])
                for kt in range(KT - 1):
                    nc.tensor.matmul(ps[:, 0:w], wb[cb][:, kt * 128:(kt + 1) * 128],
                                     xnT[:, kt * XW + r0: kt * XW + r1],
                                     start=(kt == 0), stop=False)
                tm = sPE.e(nc.tensor.matmul(ps[:, 0:w],
                                     wb[cb][:, (KT - 1) * 128:KT * 128],
                                     xnT[:, (KT - 1) * XW + r0: (KT - 1) * XW + r1],
                                     start=False, stop=True))
                wt(nc.vector, tm)
                dst0 = (0, 512, 1024)[ciH]
                pj_free[ciH % 2] = sV.e(nc.vector.tensor_copy(
                    xb[cb][:, dst0:dst0 + w], ps[:, 0:w]))
            wfree[cb] = sV.tok()
            wt(nc.vector, cv_free[cb])
            nc.vector.tensor_scalar(cv[cb][:, :], xb[cb][:, 0:T], cwcol[:, ti:ti + 1],
                                    None, ALU.mult)
            tcv = None
            for k in range(1, 4):
                tcv = sV.e(nc.vector.scalar_tensor_tensor(cv[cb][:, :], xb[cb][:, k:k + T],
                                               cwcol[:, k * 34 + ti:k * 34 + ti + 1],
                                               cv[cb][:, :], ALU.mult, ALU.add))
            xb_free[cb] = tcv
            wt(nc.scalar, tcv)
            if ti <= 31:
                wt(nc.scalar, cvb_free[cb])
                ta = sA.e(nc.scalar.activation(cvb[cb][:, :], cv[cb][:, :], AF.Silu,
                                               bias=cbcol[:, ti:ti + 1]))
                cv_free[cb] = ta
                wt(nc.tensor, ta)
                for k in range(NCH):
                    sl = k % 4
                    wt(nc.tensor, slotT[sl])
                    tm = sPE.e(nc.tensor.transpose(ptrbs[sl][:, :],
                                                   cvb[cb][:, k * CH:(k + 1) * CH],
                                                   identb[:, :]))
                    wt(nc.vector, tm)
                    slotT[sl] = sV.e(nc.vector.tensor_copy(
                        XY[:, k * DI + ti * 128:k * DI + (ti + 1) * 128],
                        ptrbs[sl][:, :]))
                cvb_free[cb] = sV.tok()
            elif ti == 32:
                ta = sA.e(nc.scalar.activation(Bk[:, :], cv[cb][:, :], AF.Silu,
                                               bias=cbcol[:, ti:ti + 1]))
                cv_free[cb] = ta
                wt(nc.tensor, ta)
                for k in range(NCH):
                    sl = k % 4
                    wt(nc.tensor, slotT[sl])
                    tm = sPE.e(nc.tensor.transpose(ptrbs[sl][:, :],
                                                   Bk[:, k * CH:(k + 1) * CH],
                                                   identb[:, :]))
                    wt(nc.vector, tm)
                    slotT[sl] = sV.e(nc.vector.tensor_copy(
                        BTk[:, k * CH:(k + 1) * CH],
                        ptrbs[sl][:, :]))
            else:
                ta = sA.e(nc.scalar.activation(Ck[:, :], cv[cb][:, :], AF.Silu,
                                               bias=cbcol[:, ti:ti + 1]))
                cv_free[cb] = ta

        zl = [None, None]
        zfree = [None, None]
        zb_free = [None, None]
        wt(nc.gpsimd, sPE.tok())
        zl[0] = sWA[0].d(nc.gpsimd.dma_start(
            wb[0][:, :].rearrange("p (k c) -> p k c", k=KT), winz_d[:, 0, :, :]))
        for ti in range(32):
            cb = ti % 2
            if ti + 1 < 32:
                wt(nc.gpsimd, zfree[(ti + 1) % 2])
                zl[(ti + 1) % 2] = sWA[(ti + 1) % 2].d(nc.gpsimd.dma_start(
                    wb[(ti + 1) % 2][:, :].rearrange("p (k c) -> p k c", k=KT),
                    winz_d[:, ti + 1, :, :]))
            wt(nc.tensor, zl[cb])
            for ci, (r0, r1) in enumerate(NC_REAL):
                ps = pj[ci % 2]
                wt(nc.tensor, pj_free[ci % 2])
                for kt in range(KT - 1):
                    nc.tensor.matmul(ps[:, :], wb[cb][:, kt * 128:(kt + 1) * 128],
                                     xnT[:, kt * XW + r0: kt * XW + r1],
                                     start=(kt == 0), stop=False)
                tm = sPE.e(nc.tensor.matmul(ps[:, :],
                                     wb[cb][:, (KT - 1) * 128:KT * 128],
                                     xnT[:, (KT - 1) * XW + r0: (KT - 1) * XW + r1],
                                     start=False, stop=True))
                wt(nc.scalar, tm)
                wt(nc.scalar, zb_free[cb] if ci == 0 else None)
                ta = sA.e(nc.scalar.activation(cvb[cb][:, ci * 512:(ci + 1) * 512],
                                               ps[:, :], AF.Silu))
                pj_free[ci % 2] = ta
            wt(nc.scalar, sA.tok())
            zb_free[cb] = sSP.d(nc.scalar.dma_start(szsp.ap()[ti, :, :], cvb[cb][:, :]))
            zfree[cb] = sPE.tok()
        wt(nc.sync, sSP.tok())
        pstB.close()
        stB.close()
        nc.all_engine_barrier()

        if debug:
            for nm, sb in (("dbg_XYx", XY), ("dbg_Bk", Bk), ("dbg_Ck", Ck),
                           ("dbg_BTk", BTk)):
                d = dbgo(nm, list(sb.shape), BF16)
                sDB.d(nc.sync.dma_start(d, sb[:, :]))
            wt(nc.sync, sDB.tok())
            nc.all_engine_barrier()

    stXN.close()

    stS = ExitStack()
    S1 = stS.enter_context(nc.sbuf_tensor("S1", [128, DI], F32))
    S1b = stS.enter_context(nc.sbuf_tensor("S1b", [128, DI], BF16))
    diffL = [stS.enter_context(nc.sbuf_tensor(f"diffL{i}", [4, NH * CH], BF16,
                                              side="right" if i else None))
             for i in range(2)]
    diffR = [stS.enter_context(nc.sbuf_tensor(f"diffR{i}", [4, NH * CH], BF16,
                                              side="right" if i else None))
             for i in range(2)]
    dtTk = stS.enter_context(nc.sbuf_tensor("dtTk", [128, 64], F32))
    ncATk = stS.enter_context(nc.sbuf_tensor("ncATk", [128, 64], F32))
    wprek = stS.enter_context(nc.sbuf_tensor("wprek", [128, 64], F32))
    wScolk = stS.enter_context(nc.sbuf_tensor("wScolk", [128, 64], F32))
    ebck = stS.enter_context(nc.sbuf_tensor("ebck", [128, 64], F32))
    Gm = stS.enter_context(nc.sbuf_tensor("Gm", [128, 128], BF16))
    Ck4 = stS.enter_context(nc.sbuf_tensor("Ck4", [128, 512], BF16))
    Dm5 = stS.enter_context(nc.sbuf_tensor("Dm5", [128, 256], F32))
    Em = stS.enter_context(nc.sbuf_tensor("Em", [128, 256], BF16))
    Mm = stS.enter_context(nc.sbuf_tensor("Mm", [128, 256], BF16))
    Etq = stS.enter_context(nc.sbuf_tensor("Etq", [128, 1024], BF16))
    CE4 = stS.enter_context(nc.sbuf_tensor("CE4", [128, 1024], BF16))
    xwb = stS.enter_context(nc.sbuf_tensor("xwb", [128, 128], BF16))

    if on("p4"):
        pstS = ExitStack()
        pDs = [pstS.enter_context(nc.psum_tensor(f"pD{i}", [128, 128], F32))
               for i in range(2)]
        pB4 = [pstS.enter_context(nc.psum_tensor(f"pB4{i}", [128, 512], F32))
               for i in range(2)]
        pYs = [pstS.enter_context(nc.psum_tensor(f"pY{i}", [128, 128], F32))
               for i in range(2)]
        pSs = [pstS.enter_context(nc.psum_tensor(f"pS{i}", [128, 64], F32))
               for i in range(2)]

        sV.e(nc.vector.memset(S1[:, :], 0.0))
        sV.e(nc.vector.memset(S1b[:, :], 0.0))
        for i in range(2):
            nc.vector.memset(diffL[i][0:4, :], 1.0)
            sV.e(nc.vector.memset(diffR[i][0:4, :], 1.0))
        tmemset = sV.tok()

        def flats(k, toks):
            par = k % 2
            sem = sFL[par]
            for tk in toks:
                wt(nc.sync, tk)
            nc.sync.dma_start(diffL[par][2:3, :], ncAhi[:, k * CH:(k + 1) * CH]).then_inc(sem.h, 16)
            nc.sync.dma_start(diffL[par][3:4, :], ncAlo[:, k * CH:(k + 1) * CH]).then_inc(sem.h, 16)
            nc.sync.dma_start(diffR[par][0:1, :], cAhi[:, k * CH:(k + 1) * CH]).then_inc(sem.h, 16)
            nc.sync.dma_start(diffR[par][1:2, :], cAlo[:, k * CH:(k + 1) * CH]).then_inc(sem.h, 16)
            sem.n += 64
            return sem.tok()

        wt(nc.sync, tmemset)
        flat_tok = [None, None]
        flat_users = [None, None]
        flat_tok[0] = flats(0, [])
        chunk_dve_done = None
        pB_free = [None, None]
        for k in range(NCH):
            par = k % 2
            kc = slice(k * CH, (k + 1) * CH)
            if k + 1 < NCH:
                flat_tok[(k + 1) % 2] = flats(k + 1, [flat_users[(k + 1) % 2]])
            wt(nc.tensor, flat_tok[par])
            wt(nc.tensor, chunk_dve_done)
            wt(nc.tensor, pB_free[0])
            wt(nc.tensor, pB_free[1])
            nc.tensor.transpose(pB4[0][:, 0:64], dtf[0:64, kc], identf[0:64, 0:64])
            nc.tensor.transpose(pB4[0][:, 64:128], cAll[0:64, kc], identf[0:64, 0:64])
            nc.tensor.matmul(pB4[0][:, 128:192], diffL[par][0:2, 0:128],
                             diffR[par][0:2, CH - 1::CH], start=True, stop=True)
            tsh = sPE.e(nc.tensor.matmul(pB4[1][:, 0:128], Bk[:, kc], Ck[:, kc],
                                         start=True, stop=True))
            wt(nc.vector, tsh)
            nc.vector.tensor_copy(dtTk[:, :], pB4[0][:, 0:64])
            tna = sV.e(nc.vector.tensor_scalar(ncATk[:, :], pB4[0][:, 64:128], -1.0,
                                               None, ALU.mult))
            wt(nc.vector, tna)
            nc.vector.tensor_tensor(wprek[:, :], pB4[0][:, 128:192], ncATk[:, :], ALU.add)
            tgm = sV.e(nc.vector.tensor_tensor(Gm[:, :], pB4[1][:, 0:128], tri[:, :],
                                               ALU.mult))
            wt(nc.scalar, tgm)
            nc.scalar.activation(ebck[:, :], pB4[0][:, 128:192], AF.Exp)
            tae = sA.e(nc.scalar.activation(wprek[:, :], wprek[:, :], AF.Exp))
            wt(nc.vector, tae)
            tws = sV.e(nc.vector.tensor_tensor(wScolk[:, :], wprek[:, :], dtTk[:, :],
                                               ALU.mult))
            tck4 = None
            for q4 in range(4):
                tck4 = sV.e(nc.vector.tensor_copy(Ck4[:, q4 * 128:(q4 + 1) * 128],
                                                  Ck[:, kc]))
            pB_free[0] = tae
            pB_free[1] = tgm
            pB_first = [tws, tgm]

            pD_free = [None, None]
            Dm_free = [None, None]
            Em_free = [None, None]
            Mm_free = [None, None]
            Et_free = [None, None]
            CE_free = [None, None]
            pY_free = [None, None]
            pS_free = [None, None]
            xw_free = None
            last_dve = None
            for h in range(NH):
                hs = h % 2
                hsl = slice(h * 128, (h + 1) * 128)
                q4 = (h // 4) % 2
                if h % 4 == 0:
                    h4sl = slice(h * 128, (h + 4) * 128)
                    wt(nc.tensor, pB_free[q4])
                    if h < 8:
                        wt(nc.tensor, pB_first[q4])
                    tb4 = sPE.e(nc.tensor.matmul(pB4[q4][:, :], diffL[par][0:2, 0:128],
                                                 diffR[par][0:2, h4sl],
                                                 start=True, stop=True))
                    wt(nc.scalar, tb4)
                    wt(nc.scalar, Et_free[q4])
                    te4 = sA.e(nc.scalar.activation(Etq[:, q4 * 512:(q4 + 1) * 512],
                                                    pB4[q4][:, :], AF.Exp))
                    pB_free[q4] = te4
                    wt(nc.vector, te4)
                    wt(nc.vector, tck4)
                    wt(nc.vector, CE_free[q4])
                    tc4 = sV.e(nc.vector.tensor_tensor(CE4[:, q4 * 512:(q4 + 1) * 512],
                                                       Etq[:, q4 * 512:(q4 + 1) * 512],
                                                       Ck4[:, :], ALU.mult))
                    Et_free[q4] = tc4
                wt(nc.tensor, pD_free[hs])
                tD = sPE.e(nc.tensor.matmul(pDs[hs][:, :],
                                            diffL[par][0:4, hsl], diffR[par][0:4, hsl],
                                            start=True, stop=True))
                wt(nc.vector, tD)
                wt(nc.vector, Dm_free[hs])
                tcl = sV.e(nc.vector.tensor_single_scalar(
                    Dm5[:, hs * 128:(hs + 1) * 128], pDs[hs][:, :],
                    0.0, ALU.min))
                pD_free[hs] = tcl
                wt(nc.scalar, tcl)
                wt(nc.scalar, Em_free[hs])
                tex = sA.e(nc.scalar.activation(Em[:, hs * 128:(hs + 1) * 128],
                                                Dm5[:, hs * 128:(hs + 1) * 128], AF.Exp))
                Dm_free[hs] = tex
                wt(nc.vector, tex)
                if h == 0:
                    wt(nc.vector, tgm)
                    wt(nc.vector, tws)
                wt(nc.vector, Mm_free[hs])
                tM0 = sV.e(nc.vector.scalar_tensor_tensor(Mm[:, hs * 128:(hs + 1) * 128],
                                               Em[:, hs * 128:(hs + 1) * 128],
                                               dtTk[:, h:h + 1], Gm[:, :],
                                               ALU.mult, ALU.mult))
                Em_free[hs] = tM0
                wt(nc.vector, tM0)
                tM = sV.e(nc.vector.scalar_tensor_tensor(
                    Mm[:, hs * 128:(hs + 1) * 128], identb[:, :], dbc[:, h:h + 1],
                    Mm[:, hs * 128:(hs + 1) * 128], ALU.mult, ALU.add))
                wt(nc.vector, xw_free)
                txw = sV.e(nc.vector.tensor_scalar(
                    xwb[:, hs * 64:(hs + 1) * 64],
                    XY[:, k * DI + h * 64:k * DI + (h + 1) * 64],
                    wScolk[:, h:h + 1], None, ALU.mult))
                ysl = (h // 2) % 2
                base = hs * 64
                if hs == 0:
                    wt(nc.tensor, pY_free[ysl])
                wt(nc.tensor, Et_free[q4] if h % 4 == 0 else None)
                nc.tensor.matmul(pYs[ysl][base:base + 64, :],
                                 S1b[:, h * 64:(h + 1) * 64],
                                 CE4[:, q4 * 512 + (h % 4) * 128: q4 * 512 + (h % 4 + 1) * 128],
                                 start=True, stop=False, tile_position=(0, base))
                wt(nc.tensor, tM)
                tYd = sPE.e(nc.tensor.matmul(
                    pYs[ysl][base:base + 64, :],
                    XY[:, k * DI + h * 64:k * DI + (h + 1) * 64],
                    Mm[:, hs * 128:(hs + 1) * 128],
                    start=False, stop=True, tile_position=(0, base)))
                Mm_free[hs] = tYd
                if h % 4 == 3:
                    CE_free[q4] = tYd
                wt(nc.tensor, txw)
                wt(nc.tensor, pS_free[hs])
                tS = sPE.e(nc.tensor.matmul(pSs[hs][:, :], BTk[:, kc],
                                            xwb[:, hs * 64:(hs + 1) * 64],
                                            start=True, stop=True))
                xw_free = tS
                wt(nc.vector, tS)
                pS_free[hs] = sV.e(nc.vector.scalar_tensor_tensor(
                                               S1[:, h * 64:(h + 1) * 64],
                                               S1[:, h * 64:(h + 1) * 64],
                                               ebck[:, h:h + 1],
                                               pSs[hs][:, :],
                                               ALU.mult, ALU.add))
                if hs == 1:
                    pi = h // 2
                    wt(nc.vector, tYd)
                    pY_free[ysl] = sV.e(nc.vector.tensor_copy(
                        XY[:, k * DI + pi * 128:k * DI + (pi + 1) * 128],
                        pYs[ysl][:, :]))
            wt(nc.vector, pS_free[1])
            last_dve = sV.e(nc.vector.tensor_copy(S1b[:, :], S1[:, :]))
            flat_users[par] = sPE.tok()
            chunk_dve_done = last_dve
        pstS.close()
        nc.all_engine_barrier()

        if debug:
            for nm, sb, dt_ in (("dbg_ylocal", XY, BF16), ("dbg_S1", S1, F32),):
                d = dbgo(nm, list(sb.shape), dt_)
                sDB.d(nc.sync.dma_start(d, sb[:, :]))
            wt(nc.sync, sDB.tok())
            nc.all_engine_barrier()

    if on("p5b"):
        cc_in = nc.dram_tensor("cc_in", [128, 4160], F32, kind="Internal")
        cc_out = nc.dram_tensor("cc_out", [NCORES * 128, 4160], F32, kind="Internal",
                                addr_space="Shared")
        with nc.psum_tensor("pr", [1, 64], F32) as pr:
            t1 = sPE.e(nc.tensor.transpose(pr[0:1, 0:64], runsumf[0:64, 0:1],
                                           identf[0:64, 0:64]))
            tms = sV.e(nc.vector.memset(Dm5[:, 0:64], 0.0))
            wt(nc.vector, t1)
            wt(nc.vector, tms)
            t2 = sV.e(nc.vector.tensor_copy(Dm5[0:1, 0:64], pr[0:1, 0:64]))
            wt(nc.sync, t2)
        sIO.d(nc.sync.dma_start(cc_in.ap()[:, 0:DI], S1[:, :]))
        tpk = sIO.d(nc.sync.dma_start(cc_in.ap()[:, DI:DI + 64], Dm5[:, 0:64]))
        wt(nc.gpsimd, tpk)
        tcc = sCC.e(nc.gpsimd.collective_compute(
            "AllGather", ALU.bypass, replica_groups=[list(range(NCORES))],
            ins=[cc_in.ap().opt()], outs=[cc_out.ap().opt()]))

        with nc.sbuf_tensor("erow", [1, NCORES * NH], F32) as erow, \
             nc.sbuf_tensor("frow", [1, NCORES * NH], F32) as frow, \
             nc.sbuf_tensor("sfx", [1, (NCORES + 1) * NH], F32) as sfx, \
             nc.sbuf_tensor("wflat", [1, NCORES * NH], F32) as wflat, \
             nc.sbuf_tensor("mrow", [1, NCORES], F32) as mrow, \
             nc.sbuf_tensor("omrow", [1, NCORES], F32) as omrow, \
             nc.sbuf_tensor("wcols", [128, NCORES * NH], F32) as wcols, \
             nc.sbuf_tensor("gbuf0", [128, 1024], F32) as gbuf0, \
             nc.sbuf_tensor("gbuf1", [128, 1024], F32) as gbuf1, \
             nc.psum_tensor("pw2", [128, 512], F32) as pw2:
            t0 = sIO.d(nc.sync.dma_start(mrow[:, :], m_d))
            wt(nc.sync, tcc)
            t1 = sIO.d(nc.sync.dma_start(
                erow[:, :].rearrange("one (r h) -> one r h", r=NCORES),
                cc_out.ap().rearrange("(r p) c -> p r c", p=128)[0:1, :, DI:DI + 64]))
            wt(nc.scalar, t1)
            ta = sA.e(nc.scalar.activation(erow[:, :], erow[:, :], AF.Exp))
            wt(nc.vector, ta)
            wt(nc.vector, t0)
            tom = sV.e(nc.vector.tensor_scalar(omrow[:, :], mrow[:, :], -1.0, 1.0,
                                               ALU.mult, ALU.add))
            wt(nc.vector, tom)
            tfr = None
            for j in range(NCORES):
                tfr = sV.e(nc.vector.tensor_scalar(frow[:, j * NH:(j + 1) * NH],
                                        erow[:, j * NH:(j + 1) * NH],
                                        mrow[0:1, j:j + 1], omrow[0:1, j:j + 1],
                                        ALU.mult, ALU.add))
            tsf = sV.e(nc.vector.memset(sfx[:, NCORES * NH:(NCORES + 1) * NH], 1.0))
            wt(nc.vector, tfr)
            for j in range(NCORES - 1, -1, -1):
                wt(nc.vector, tsf)
                tsf = sV.e(nc.vector.tensor_tensor(sfx[:, j * NH:(j + 1) * NH],
                                        sfx[:, (j + 1) * NH:(j + 2) * NH],
                                        frow[:, j * NH:(j + 1) * NH], ALU.mult))
            wt(nc.vector, tsf)
            tw2 = None
            for j in range(NCORES):
                tw2 = sV.e(nc.vector.tensor_scalar(wflat[:, j * NH:(j + 1) * NH],
                                        sfx[:, (j + 1) * NH:(j + 2) * NH],
                                        mrow[0:1, j:j + 1], None, ALU.mult))
            wt(nc.tensor, tw2)
            tm = sPE.e(nc.tensor.matmul(pw2[:, :], onesf[0:1, :], wflat[0:1, 0:512],
                             start=True, stop=True))
            wt(nc.vector, tm)
            sV.e(nc.vector.tensor_copy(wcols[:, :], pw2[:, :]))
            tz = sV.e(nc.vector.memset(S1[:, :], 0.0))
            gb = [gbuf0, gbuf1]
            gld = [None, None]
            gfree2 = [None, None]
            steps = [(j, qt) for j in range(NCORES) for qt in range(4)]
            gld[0] = sG[0].d(nc.sync.dma_start(gbuf0[:, :],
                                               cc_out.ap()[0:128, 0:1024]))
            wt(nc.vector, tz)
            for si, (j, qt) in enumerate(steps):
                if si + 1 < len(steps):
                    jn, qn = steps[si + 1]
                    wt(nc.sync, gfree2[(si + 1) % 2])
                    gld[(si + 1) % 2] = sG[(si + 1) % 2].d(nc.sync.dma_start(
                        gb[(si + 1) % 2][:, :],
                        cc_out.ap()[jn * 128:(jn + 1) * 128, qn * 1024:(qn + 1) * 1024]))
                wt(nc.vector, gld[si % 2])
                tg16 = None
                for h16 in range(16):
                    h = qt * 16 + h16
                    tg16 = sV.e(nc.vector.scalar_tensor_tensor(
                        S1[:, h * HP:(h + 1) * HP], gb[si % 2][:, h16 * HP:(h16 + 1) * HP],
                        wcols[:, j * NH + h:j * NH + h + 1], S1[:, h * HP:(h + 1) * HP],
                        ALU.mult, ALU.add))
                gfree2[si % 2] = tg16
        nc.all_engine_barrier()

        if debug:
            d = dbgo("dbg_S2", [128, DI], F32)
            sDB.d(nc.sync.dma_start(d, S1[:, :]))
            wt(nc.sync, sDB.tok())
            nc.all_engine_barrier()

    if on("p5c"):
        pstC = ExitStack()
        pB4c = [pstC.enter_context(nc.psum_tensor(f"pB4c{i}", [128, 512], F32))
                for i in range(2)]
        pCs = [pstC.enter_context(nc.psum_tensor(f"pC{i}", [128, 128], F32))
               for i in range(2)]

        flatc_tok = [None, None]
        flatc_users = [None, None]

        def flatsc(k, toks):
            par = k % 2
            sem = sFL[par]
            for tk in toks:
                wt(nc.sync, tk)
            nc.sync.dma_start(diffR[par][0:1, :], cAhi[:, k * CH:(k + 1) * CH]).then_inc(sem.h, 16)
            nc.sync.dma_start(diffR[par][1:2, :], cAlo[:, k * CH:(k + 1) * CH]).then_inc(sem.h, 16)
            sem.n += 32
            return sem.tok()

        flatc_tok[0] = flatsc(0, [])
        scale_done = None
        last_pC = None
        pB_free = [None, None]
        Et_free = [None, None]
        CE_free = [None, None]
        for k in range(NCH):
            par = k % 2
            kc = slice(k * CH, (k + 1) * CH)
            if k + 1 < NCH:
                flatc_tok[(k + 1) % 2] = flatsc(k + 1, [flatc_users[(k + 1) % 2]])
            wt(nc.vector, scale_done)
            wt(nc.vector, last_pC)
            tcast = sV.e(nc.vector.tensor_copy(S1b[:, :], S1[:, :]))
            wt(nc.tensor, flatc_tok[par])
            wt(nc.tensor, pB_free[0])
            teb = sPE.e(nc.tensor.matmul(pB4c[0][:, 128:192], diffL[par][0:2, 0:128],
                                         diffR[par][0:2, CH - 1::CH],
                                         start=True, stop=True))
            wt(nc.scalar, teb)
            tebe = sA.e(nc.scalar.activation(ebck[:, :], pB4c[0][:, 128:192], AF.Exp))
            pB_free[0] = tebe
            tck4_c = None
            for q4 in range(4):
                tck4_c = sV.e(nc.vector.tensor_copy(Ck4[:, q4 * 128:(q4 + 1) * 128],
                                                    Ck[:, kc]))

            pC_free = [None, None]
            for h in range(NH):
                hs = h % 2
                q4 = (h // 4) % 2
                if h % 4 == 0:
                    h4sl = slice(h * 128, (h + 4) * 128)
                    wt(nc.tensor, pB_free[q4])
                    tb4 = sPE.e(nc.tensor.matmul(pB4c[q4][:, :], diffL[par][0:2, 0:128],
                                                 diffR[par][0:2, h4sl],
                                                 start=True, stop=True))
                    wt(nc.scalar, tb4)
                    wt(nc.scalar, Et_free[q4])
                    te4 = sA.e(nc.scalar.activation(Etq[:, q4 * 512:(q4 + 1) * 512],
                                                    pB4c[q4][:, :], AF.Exp))
                    pB_free[q4] = te4
                    wt(nc.vector, te4)
                    wt(nc.vector, tck4_c)
                    wt(nc.vector, CE_free[q4])
                    tc4 = sV.e(nc.vector.tensor_tensor(CE4[:, q4 * 512:(q4 + 1) * 512],
                                                       Etq[:, q4 * 512:(q4 + 1) * 512],
                                                       Ck4[:, :], ALU.mult))
                    Et_free[q4] = tc4
                base = hs * 64
                ysl = (h // 2) % 2
                if h == 0:
                    wt(nc.tensor, tcast)
                if hs == 0:
                    wt(nc.tensor, pC_free[ysl])
                wt(nc.tensor, Et_free[q4] if h % 4 == 0 else None)
                tCm = sPE.e(nc.tensor.matmul(
                    pCs[ysl][base:base + 64, :],
                    S1b[:, h * 64:(h + 1) * 64],
                    CE4[:, q4 * 512 + (h % 4) * 128: q4 * 512 + (h % 4 + 1) * 128],
                    start=True, stop=True, tile_position=(0, base)))
                if h % 4 == 3:
                    CE_free[q4] = tCm
                last_pC = tCm
                if hs == 1:
                    pi = h // 2
                    wt(nc.vector, tCm)
                    pC_free[ysl] = sV.e(nc.vector.tensor_tensor(
                        XY[:, k * DI + pi * 128:k * DI + (pi + 1) * 128],
                        XY[:, k * DI + pi * 128:k * DI + (pi + 1) * 128],
                        pCs[ysl][:, :], ALU.add))
            wt(nc.vector, tebe)
            scale_done = None
            for h in range(NH):
                scale_done = sV.e(nc.vector.tensor_scalar(S1[:, h * HP:(h + 1) * HP],
                                        S1[:, h * HP:(h + 1) * HP],
                                        ebck[:, h:h + 1], None, ALU.mult))
            flatc_users[par] = sPE.tok()
        pstC.close()
        nc.all_engine_barrier()

        if debug:
            d = dbgo("dbg_yfull", [128, NCH * DI], BF16)
            sDB.d(nc.sync.dma_start(d, XY[:, :]))
            wt(nc.sync, sDB.tok())
            nc.all_engine_barrier()

    stS.close()

    try:
      if on("p6a"):
        XYr = XY[:, :].rearrange("p (k f l) -> p k f l", f=32, l=128)
        stP6 = ExitStack()
        szb = [stP6.enter_context(nc.sbuf_tensor(f"szb{i}", [128, T], BF16))
               for i in range(2)]
        sqb = stP6.enter_context(nc.sbuf_tensor("sqb", [128, T], BF16))
        vrow = stP6.enter_context(nc.sbuf_tensor("vrow", [1, T], F32))
        pstP = ExitStack()
        pSS = pstP.enter_context(nc.psum_tensor("pSS", [1, T], F32))
        pRB = pstP.enter_context(nc.psum_tensor("pRB", [128, T], F32))

        szld = [None, None]
        szfree = [None, None]
        gate_done = [None] * 32
        sq_done = None
        szld[0] = sZ[0].d(nc.scalar.dma_start(szb[0][:, :], szsp.ap()[0, :, :]))
        for ft in range(32):
            cb = ft % 2
            if ft + 1 < 32:
                wt(nc.scalar, szfree[(ft + 1) % 2])
                szld[(ft + 1) % 2] = sZ[(ft + 1) % 2].d(nc.scalar.dma_start(
                    szb[(ft + 1) % 2][:, :], szsp.ap()[ft + 1, :, :]))
            wt(nc.vector, szld[cb])
            tg = sV.e(nc.vector.tensor_tensor(XYr[:, :, ft, :], XYr[:, :, ft, :],
                                              szb[cb][:, :].rearrange("p (k l) -> p k l", l=128),
                                              ALU.mult))
            szfree[cb] = tg
            gate_done[ft] = tg
            wt(nc.scalar, tg)
            wt(nc.scalar, sq_done)
            tsq = sA.e(nc.scalar.activation(sqb[:, :].rearrange("p (k l) -> p k l", l=128),
                                            XYr[:, :, ft, :], AF.Square))
            wt(nc.tensor, tsq)
            nc.tensor.matmul(pSS[0:1, 0:512], onescol[:, 0:1],
                             sqb[:, 0:512],
                             start=(ft == 0), stop=(ft == 31))
            sq_done = sPE.e(nc.tensor.matmul(pSS[0:1, 512:1024], onescol[:, 0:1],
                             sqb[:, 512:1024],
                             start=(ft == 0), stop=(ft == 31)))
        if not on("p6b"):
            wt(nc.vector, sq_done)
            wt(nc.scalar, sA.tok())
            stP6.close()
            pstP.close()
            raise _EndP6()
        wt(nc.vector, sq_done)
        tvr = sV.e(nc.vector.tensor_scalar(vrow[:, :], pSS[0:1, :], 1.0 / DI, EPS,
                                           ALU.mult, ALU.add))
        wt(nc.scalar, tvr)
        nc.scalar.activation(vrow[:, :], vrow[:, :], AF.Ln)
        trs = sA.e(nc.scalar.activation(vrow[:, :], vrow[:, :], AF.Exp, scale=-0.5))
        wt(nc.tensor, trs)
        nc.tensor.matmul(pRB[:, 0:512], onesf[0:1, :],
                         vrow[0:1, 0:512], start=True, stop=True)
        trb = sPE.e(nc.tensor.matmul(pRB[:, 512:1024], onesf[0:1, :],
                         vrow[0:1, 512:1024], start=True, stop=True))
        wt(nc.vector, trb)
        yn_done = [None] * 32
        for ft in range(32):
            yn_done[ft] = sV.e(nc.vector.scalar_tensor_tensor(
                XYr[:, :, ft, :], XYr[:, :, ft, :], nwcol[:, ft:ft + 1],
                pRB[:, :].rearrange("p (k l) -> p k l", l=128), ALU.mult, ALU.mult))
        tynorm = sV.tok()

        if not on("p6"):
            wt(nc.vector, tynorm)
            stP6.close()
            pstP.close()
            raise _EndP6()
        stW = ExitStack()
        wob = [stW.enter_context(nc.sbuf_tensor(f"wob{i}", [128, 4 * 32 * 128], BF16))
               for i in range(2)]
        gsl = stW.enter_context(nc.sbuf_tensor("gsl", [128, 4 * 512], BF16))
        xbl = stW.enter_context(nc.sbuf_tensor("xbl", [128, 4 * 512], F32))
        sgm = stW.enter_context(nc.sbuf_tensor("sgm", [128, 512], F32))
        ost = [stW.enter_context(nc.sbuf_tensor(f"ost{i}", [128, 512], F32))
               for i in range(4)]
        pstP.close()
        pO = [pstP.enter_context(nc.psum_tensor(f"pO{i}", [128, 512], F32))
              for i in range(4)]
        pT = [pstP.enter_context(nc.psum_tensor(f"pT{i}", [128, 128], F32))
              for i in range(2)]
        pT_free = [None, None]
        wo_ld = [None, None]
        wo_free = [None, None]
        tall = None
        ttr_last = None
        wo_ld[0] = sWA[0].d(nc.gpsimd.dma_start(
            wob[0][:, :].rearrange("p (ct k c) -> p ct k c", ct=4, k=32),
            wo_d[:, 0:4, :, :]))
        for q in range(4):
            cb = q % 2
            if q + 1 < 4:
                wt(nc.gpsimd, wo_free[(q + 1) % 2])
                wo_ld[(q + 1) % 2] = sWA[(q + 1) % 2].d(nc.gpsimd.dma_start(
                    wob[(q + 1) % 2][:, :].rearrange("p (ct k c) -> p ct k c", ct=4, k=32),
                    wo_d[:, (q + 1) * 4:(q + 2) * 4, :, :]))
            wt(nc.tensor, wo_ld[cb])
            for ci in range(2):
                wt(nc.sync, tall)
                tgl = sIO.d(nc.sync.dma_start(
                    gsl[:, :].rearrange("p (dd c) -> p dd c", dd=4),
                    gsp.ap()[q * 512:(q + 1) * 512, ci * 512:(ci + 1) * 512]
                    .rearrange("(dd p) c -> p dd c", p=128)))
                txl = sIO.d(nc.sync.dma_start(
                    xbl[:, :].rearrange("p (i c) -> p i c", i=4),
                    xw_d[128 + ci * 512:128 + (ci + 1) * 512, q * 512:(q + 1) * 512]
                    .rearrange("(i p) c -> p i c", p=128)))
                wt(nc.tensor, tall)
                if q == 0 and ci == 0:
                    wt(nc.tensor, tynorm)
                tmm = None
                for dd in range(4):
                    for kk in range(32):
                        mm = nc.tensor.matmul(pO[dd][:, :],
                                         wob[cb][:, (dd * 32 + kk) * 128:(dd * 32 + kk + 1) * 128],
                                         XYr[:, ci * 4:(ci + 1) * 4, kk, :],
                                         start=(kk == 0), stop=(kk == 31))
                        if dd == 3 and kk == 31:
                            tmm = sPE.e(mm)
                wt(nc.vector, tmm)
                wt(nc.vector, txl)
                wt(nc.vector, sST.tok())
                for dd in range(4):
                    wt(nc.vector, ttr_last)
                    tsg = sV.e(nc.vector.tensor_tensor(sgm[:, :], pO[dd][:, :],
                                                       gsl[:, dd * 512:(dd + 1) * 512],
                                                       ALU.mult))
                    if not on("p6"):
                        wt(nc.scalar, tsg)
                        sST.d(nc.scalar.dma_start(
                            out_d[dd * 128:(dd + 1) * 128, q * 512:(q + 1) * 512],
                            sgm[:, :]))
                        ttr_last = tsg
                        continue
                    wt(nc.tensor, tsg)
                    for i in range(4):
                        sl = i % 2
                        wt(nc.tensor, pT_free[sl])
                        ttr = sPE.e(nc.tensor.transpose(pT[sl][:, :],
                                                        sgm[:, i * 128:(i + 1) * 128],
                                                        identf[:, :]))
                        wt(nc.vector, ttr)
                        pT_free[sl] = sV.e(nc.vector.tensor_add(
                            ost[i][:, dd * 128:(dd + 1) * 128], pT[sl][:, :],
                            xbl[:, i * 512 + dd * 128: i * 512 + (dd + 1) * 128]))
                        if dd == 3 and i == 3:
                            tall = pT_free[sl]
                    ttr_last = ttr
                wt(nc.sync, tall)
                for i in range(4):
                    sST.d(nc.sync.dma_start(
                        out_d[ci * 512 + i * 128: ci * 512 + (i + 1) * 128,
                              q * 512:(q + 1) * 512], ost[i][:, :]))
            wo_free[cb] = sPE.tok()
        wt(nc.sync, sST.tok())
        stW.close()
        pstP.close()
        stP6.close()

    except _EndP6:
        pass
    for sem in (sConst, sCst, sX[0], sX[1], sIO, sFL[0], sFL[1], sG[0], sG[1],
                sDB, sST):
        wt(nc.sync, sem.tok())
    for sem in (sZ[0], sZ[1], sSP):
        wt(nc.scalar, sem.tok())
    for sem in (sWA[0], sWA[1], sWB[0], sWB[1]):
        wt(nc.gpsimd, sem.tok())
    stXY.close()
    stSSD.close()
    st0.close()
    return nc, dbg


_SHARED = None


def shared_layout(inputs):
    global _SHARED
    if _SHARED is not None:
        return _SHARED
    W_in = np.ascontiguousarray(inputs["L0_attn_W_in"], dtype=np.float32)
    u8 = lambda a: np.asarray(a, dtype=np.int32).astype(np.uint8)

    def slab(w, nct, cw):
        K = w.shape[0]
        return np.ascontiguousarray(
            w.reshape(K // 128, 128, nct, cw).transpose(1, 2, 0, 3))

    wgT = u8(inputs["L0_ffn_wq_gate"]).T
    wuT = u8(inputs["L0_ffn_wq_up"]).T
    wdT = u8(inputs["L0_ffn_wq_down"]).T
    _SHARED = {
        "winz": slab(W_in[:, :DI], 32, 128),
        "winxbc": slab(W_in[:, DI:DI + 4352], 34, 128),
        "windt": slab(W_in[:, DI + 4352:], 1, 64),
        "wg4": slab(wgT, NHT, 128),
        "wu4": slab(wuT, NHT, 128),
        "wd4": slab(wdT, KT, 128),
        "wo4": slab(np.ascontiguousarray(inputs["L0_attn_W_out"], dtype=np.float32),
                    16, 128),
        "conv_wt": np.ascontiguousarray(inputs["L0_attn_conv_w"].T),
        "conv_b": np.ascontiguousarray(np.asarray(inputs["L0_attn_conv_b"]).reshape(1, -1)),
        "a_vec": np.ascontiguousarray(np.asarray(inputs["L0_attn_A"]).reshape(1, -1)),
        "d_vec": np.ascontiguousarray(np.asarray(inputs["L0_attn_D"]).reshape(1, -1)),
        "dt_bias": np.ascontiguousarray(np.asarray(inputs["L0_attn_dt_bias"]).reshape(1, -1)),
        "norm_w": np.ascontiguousarray(np.asarray(inputs["L0_attn_norm_w"]).reshape(32, 128)),
        "s_gate": np.ascontiguousarray(np.asarray(inputs["L0_ffn_s_gate"]).reshape(NHT, 128)),
        "s_up": np.ascontiguousarray(np.asarray(inputs["L0_ffn_s_up"]).reshape(NHT, 128)),
        "s_down": np.ascontiguousarray(np.asarray(inputs["L0_ffn_s_down"]).reshape(16, 128)),
    }
    return _SHARED


def core_layout(inputs, core):
    x = np.asarray(inputs["L0_attn_hidden_states"][0], dtype=np.float32)
    cs = core * T
    if core == 0:
        xwin = np.concatenate([np.zeros((128, DM), np.float32), x[0:T]], axis=0)
    else:
        xwin = np.ascontiguousarray(x[cs - 128:cs + T])
    mask = np.zeros((1, NCORES), np.float32)
    mask[0, :core] = 1.0
    return {"xw": xwin, "mask": mask}


def kernel(**inputs):
    global _SHARED
    _SHARED = None
    nc, _ = build(debug=False)
    shared = shared_layout(inputs)
    in_maps = []
    for c in range(NCORES):
        m = dict(shared)
        m.update(core_layout(inputs, c))
        in_maps.append(m)
    res = run_bass_kernel_spmd(nc, in_maps, core_ids=list(range(NCORES)))
    out = np.concatenate([res.results[c]["out"] for c in range(NCORES)], axis=0)
    return out.reshape(1, NCORES * T, DM)
